# version 01 (final)
# baseline (speedup 1.0000x reference)
import contextlib
import numpy as np

import concourse.bass as bass
import concourse.mybir as mybir
from concourse.bass import AP
from concourse.bass_utils import run_bass_kernel_spmd

F32 = mybir.dt.float32
BF16 = mybir.dt.bfloat16
AL = mybir.AluOpType
AF = mybir.ActivationFunctionType
AX = mybir.AxisListType

NCORES = 8
SEQ = 2
L = 4096
DM = 1024
DI = 2048
DS = 16
DTR = 64
NT = DI // 128
NF = DM // 128
TOK = SEQ * L
TP = 512
NCHP = L // TP
TC = 1024
NSC = L // TC
SG = 8
SW = 2
IFF = 3072
NTI = IFF // 128
QMAX = 32767.0
RMAGIC = float(1.5 * 2 ** 23)
EPS = 1e-5

ENGINES = ("pe", "act", "dve", "sp", "gp")

DBG = ()
PHASES = "abcf"
FFSUB = 9
import os
P1MODE = int(os.environ.get("P1MODE", "4"))


class Sched:
    def __init__(self):
        self.q = {e: [] for e in ENGINES}
        self.cnt = {e: 0 for e in ENGINES}
        self.hw = {}
        self.waited = {e: set() for e in ENGINES}

    def op(self, eng, fn, deps=(), reads=(), writes=()):
        dl = list(deps)
        for b in reads:
            dl += b.read_deps()
        for b in writes:
            dl += b.write_deps()
        for (de, dc) in dl:
            if dc <= 0:
                continue
            if de == eng:
                if eng == "pe":
                    continue
                if dc <= self.hw.get((eng, de), 0):
                    continue
                self.hw[(eng, de)] = dc
                self.waited[de].add(dc)
                self.q[eng].append(("wait", de, dc))
                continue
            if dc <= self.hw.get((eng, de), 0):
                continue
            self.hw[(eng, de)] = dc
            self.waited[de].add(dc)
            self.q[eng].append(("wait", de, dc))
        self.cnt[eng] += 1
        tok = (eng, self.cnt[eng])
        self.q[eng].append(("op", fn, self.cnt[eng]))
        for b in reads:
            b.note_read(tok)
        for b in writes:
            b.note_write(tok)
        return tok

    def barrier(self):
        snap = dict(self.cnt)
        for w in ENGINES:
            for e in ENGINES:
                if e == w or snap[e] == 0 or snap[e] <= self.hw.get((w, e), 0):
                    continue
                self.hw[(w, e)] = snap[e]
                self.waited[e].add(snap[e])
                self.q[w].append(("wait", e, snap[e]))

    def final_wait(self, eng, waitee):
        c = self.cnt[waitee]
        if c > 0:
            self.hw[(eng, waitee)] = c
            self.waited[waitee].add(c)
            self.q[eng].append(("wait", waitee, c))

    def emit(self, block, sems):
        incmap = {}
        for e in ENGINES:
            incmap[e] = {c: i + 1 for i, c in enumerate(sorted(self.waited[e]))}
        scale = {e: (16 if e == "sp" else 1) for e in ENGINES}

        def run_queue(engobj, e):
            incs = incmap[e]
            for item in self.q[e]:
                if item[0] == "wait":
                    _, de, dc = item
                    if de == "sp":
                        engobj.wait_ge(sems[de], dc * 16)
                        continue
                    m = incmap[de]
                    v = m.get(dc)
                    if v is None:
                        cands = [c for c in m if c >= dc]
                        if not cands:
                            continue
                        v = m[min(cands)]
                    engobj.wait_ge(sems[de], v * scale[de])
                else:
                    _, fn, c = item
                    inst = fn(engobj)
                    if e == "sp" or c in incs:
                        inst.then_inc(sems[e], scale[e])

        @block.tensor
        def _(t):
            run_queue(t, "pe")

        @block.scalar
        def _(s):
            run_queue(s, "act")

        @block.vector
        def _(v):
            run_queue(v, "dve")

        @block.sync
        def _(s):
            run_queue(s, "sp")

        @block.gpsimd
        def _(g):
            run_queue(g, "gp")


class Buf:
    __slots__ = ("wr", "rd", "excl")

    def __init__(self, excl=False):
        self.wr = None
        self.rd = []
        self.excl = excl

    def read_deps(self):
        d = [self.wr] if self.wr else []
        if self.excl:
            d += self.rd
        return d

    def write_deps(self):
        return (list(self.rd) + [self.wr]) if self.wr else list(self.rd)

    def note_read(self, tok):
        self.rd.append(tok)

    def note_write(self, tok):
        self.wr = tok
        self.rd = []


def rev_free(ap2d, n=None):
    raw = [list(x) for x in ap2d.ap]
    assert len(raw) == 2 and raw[1][0] == 1, raw
    nn = raw[1][1] if n is None else n
    return AP(ap2d.tensor, ap2d.offset + (nn - 1), [raw[0], [-1, nn]])


def build_program():
    nc = bass.Bass("TRN2", target_bir_lowering=False)

    x_in = nc.dram_tensor("x_in", [TOK, DM], F32, kind="ExternalInput").ap()
    w_inT = nc.dram_tensor("w_inT", [DM, 2 * DI], F32, kind="ExternalInput").ap()
    w_outT = nc.dram_tensor("w_outT", [DI, DM], F32, kind="ExternalInput").ap()
    iv_dir = {}
    for d in ("f", "r"):
        iv_dir[d] = dict(
            w_xT=nc.dram_tensor(f"w_xT_{d}", [DI, 96], F32, kind="ExternalInput").ap(),
            w_dtT=nc.dram_tensor(f"w_dtT_{d}", [DTR, DI], F32, kind="ExternalInput").ap(),
            conv_w=nc.dram_tensor(f"conv_w_{d}", [DI, 4], F32, kind="ExternalInput").ap(),
            conv_b=nc.dram_tensor(f"conv_b_{d}", [DI, 1], F32, kind="ExternalInput").ap(),
            b_dt=nc.dram_tensor(f"b_dt_{d}", [DI, 1], F32, kind="ExternalInput").ap(),
            A_log=nc.dram_tensor(f"A_log_{d}", [DI, DS], F32, kind="ExternalInput").ap(),
            Dv=nc.dram_tensor(f"D_{d}", [DI, 1], F32, kind="ExternalInput").ap(),
        )
    w_gate = nc.dram_tensor("w_gate", [IFF, DM], F32, kind="ExternalInput").ap()
    w_up = nc.dram_tensor("w_up", [IFF, DM], F32, kind="ExternalInput").ap()
    w_down = nc.dram_tensor("w_down", [DM, IFF], F32, kind="ExternalInput").ap()
    out = nc.dram_tensor("out", [TOK, DM], F32, kind="ExternalOutput").ap()

    def idram(name, shape, dt):
        return nc.dram_tensor(name, shape, dt, kind="Internal").ap()

    sp_winT = idram("sp_winT", [2 * DI // 128, NF, 128, 128], BF16)
    sp_woutT = idram("sp_woutT", [NF, NT, 128, 128], BF16)
    sp_xnT = idram("sp_xnT", [NF, 128, TOK], BF16)
    sp_xh = idram("sp_xh", [SEQ, NT, 128, L], BF16)
    sp_z = idram("sp_z", [SEQ, NT, 128, L], BF16)
    sp_u = idram("sp_u", [4, NT, 128, L], BF16)
    sp_dt = idram("sp_dt", [4, NT, 128, L], BF16)
    sp_B = idram("sp_B", [4, DS, L], BF16)
    sp_C = idram("sp_C", [4, DS, L], BF16)
    sp_attn = idram("sp_attn", [SEQ, NF, 128, L], F32)
    sp_yg = idram("sp_yg", [4, NT, 128, L], BF16)
    sp_upT = idram("sp_upT", [NF, NTI, 128, 128], BF16)
    sp_gateT = idram("sp_gateT", [NF, NTI, 128, 128], BF16)
    sp_downT = idram("sp_downT", [NTI, NF, 128, 128], BF16)
    sp_up = idram("sp_up", [NTI, 128, TOK], BF16)
    sp_gate = idram("sp_gate", [NTI, 128, TOK], BF16)
    sp_m = idram("sp_m", [NTI, 128, TOK], BF16)
    sp_t = idram("sp_t", [NTI, 128, TOK], BF16)
    ccb = {k: (idram(f"cc{k}_i", [1, 128], F32), idram(f"cc{k}_o", [1, 128], F32))
           for k in ("x", "ug", "gg", "m", "t")}

    dbg_out = {}
    spmap = {"xnT": sp_xnT, "xh": sp_xh, "z": sp_z, "u": sp_u, "dt": sp_dt,
             "B": sp_B, "C": sp_C, "attn": sp_attn, "up": sp_up,
             "gate": sp_gate, "m": sp_m, "t": sp_t, "yg": sp_yg}
    for name in DBG:
        src = spmap[name]
        dbg_out[name] = nc.dram_tensor("dbg_" + name, list(src.shape),
                                       src.dtype, kind="ExternalOutput").ap()

    S = Sched()
    bufs = {}

    def B(name):
        if name not in bufs:
            bufs[name] = Buf(excl=name.startswith(("ps", "p0ps")))
        return bufs[name]

    def dma(o, i, **kw):
        return lambda e: e.dma_start(out=o, in_=i, **kw)

    with contextlib.ExitStack() as ctx:
        sems = {e: ctx.enter_context(nc.semaphore("sem_" + e)) for e in ENGINES}

        sb = lambda name, shape, dt: ctx.enter_context(nc.sbuf_tensor(name, shape, dt))
        identbf = sb("identbf", [128, 128], BF16)
        ident32 = sb("ident32", [128, 128], F32)
        vconv = sb("vconv", [128, 2, NT, 4], F32)
        vconvb = sb("vconvb", [128, 2, NT], F32)
        vbdt = sb("vbdt", [128, 2, NT], F32)
        vD = sb("vD", [128, 2, NT], F32)
        veA = sb("veA", [128, 2, NT, DS], BF16)
        scal = sb("scal", [128, 24], F32)
        srow = sb("srow", [1, 128], F32)
        xmaxr = sb("xmaxr", [128, 8], F32)

        for t32, bn in ((identbf, "idbf"), (ident32, "id32")):
            S.op("gp", lambda e, t=t32: e.memset(t.ap(), 0.0), writes=[B(bn)])
            S.op("gp", lambda e, t=t32: e.affine_select(
                t.ap(), t.ap(), pattern=[[-1, 128]], compare_op=AL.not_equal,
                fill=1.0, base=0, channel_multiplier=1), writes=[B(bn)])

        for di, d in enumerate(("f", "r")):
            iv = iv_dir[d]
            S.op("sp", dma(vconv.ap()[:, di],
                           iv["conv_w"].rearrange("(t p) k -> p t k", p=128)),
                 writes=[B("vconv")])
            S.op("sp", dma(vconvb.ap()[:, di],
                           iv["conv_b"].rearrange("(t p) o -> p (t o)", p=128),
                           allow_slow_non_contiguous=True),
                 writes=[B("vconvb")])
            S.op("sp", dma(vbdt.ap()[:, di],
                           iv["b_dt"].rearrange("(t p) o -> p (t o)", p=128),
                           allow_slow_non_contiguous=True),
                 writes=[B("vbdt")])
            S.op("sp", dma(vD.ap()[:, di],
                           iv["Dv"].rearrange("(t p) o -> p (t o)", p=128),
                           allow_slow_non_contiguous=True),
                 writes=[B("vD")])

        ph0_es = ctx.enter_context(contextlib.ExitStack())
        p0a = ph0_es.enter_context(nc.sbuf_tensor("p0_a", [128, 2 * DI], F32))
        p0q = ph0_es.enter_context(nc.sbuf_tensor("p0_q", [128, 2 * DI], F32))
        p0b = ph0_es.enter_context(nc.sbuf_tensor("p0_b", [128, 2 * DI], BF16))
        p0s = ph0_es.enter_context(nc.sbuf_tensor("p0_s", [128, 96], F32))
        p0r = ph0_es.enter_context(nc.sbuf_tensor("p0_r", [128, 96], F32))
        p0e = ph0_es.enter_context(nc.sbuf_tensor("p0_e", [128, 128], BF16))
        p0ps = ph0_es.enter_context(nc.psum_tensor("p0_ps", [128, 512], BF16))
        if True:

            for di, d in enumerate(("f", "r")):
                iv = iv_dir[d]
                for t in range(NT):
                    S.op("sp", dma(p0a.ap()[:, 0:DS], iv["A_log"][128 * t:128 * (t + 1), :]),
                         writes=[B("p0a")])
                    S.op("act", lambda e, di=di, t=t: e.activation(
                        out=veA.ap()[:, di, t], in_=p0a.ap()[:, 0:DS], func=AF.Exp),
                        reads=[B("p0a")], writes=[B("veA")])

            def cast_spill_lhsT(src, dst, R, Cfree):
                for kt in range(R // 128):
                    S.op("sp", dma(p0a.ap()[:, 0:Cfree], src[128 * kt:128 * (kt + 1), :]),
                         writes=[B("p0a")])
                    S.op("dve", lambda e, C=Cfree: e.tensor_copy(
                        p0b.ap()[:, 0:C], p0a.ap()[:, 0:C]),
                        reads=[B("p0a")], writes=[B("p0b")])
                    for mt in range(Cfree // 128):
                        S.op("sp", dma(dst[mt, kt], p0b.ap()[:, 128 * mt:128 * (mt + 1)]),
                             reads=[B("p0b")], writes=[B("spw")])

            cast_spill_lhsT(w_inT, sp_winT, DM, 2 * DI)
            cast_spill_lhsT(w_outT, sp_woutT, DI, DM)

            def dequant_transpose(src, dst, O, I, kt_first):
                nob = I // 32
                for ot in range(O // 128):
                    S.op("sp", dma(p0a.ap()[:, 0:I], src[128 * ot:128 * (ot + 1), :]),
                         writes=[B("p0a")])
                    blkview = AP(p0a.ap().tensor, 0, [[2 * DI, 128], [32, nob], [1, 32]])
                    S.op("dve", lambda e, blkview=blkview, nob=nob: e.tensor_reduce(
                        p0s.ap()[:, 0:nob], blkview, AX.X, AL.max,
                        apply_absolute_value=True),
                        reads=[B("p0a")], writes=[B("p0s")])
                    S.op("dve", lambda e, nob=nob: e.tensor_scalar(
                        p0s.ap()[:, 0:nob], p0s.ap()[:, 0:nob],
                        1.0 / 7.0, 1e-12, AL.mult, AL.max),
                        reads=[B("p0s")], writes=[B("p0s")])
                    S.op("dve", lambda e, nob=nob: e.reciprocal(
                        p0r.ap()[:, 0:nob], p0s.ap()[:, 0:nob]),
                        reads=[B("p0s")], writes=[B("p0r")])
                    rb = AP(p0r.ap().tensor, 0, [[96, 128], [1, nob], [0, 32]])
                    S.op("dve", lambda e, rb=rb, I=I: e.tensor_tensor(
                        p0q.ap()[:, 0:I].rearrange("p (a b) -> p a b", b=32),
                        AP(p0a.ap().tensor, 0, [[2 * DI, 128], [32, nob], [1, 32]]),
                        rb, AL.mult),
                        reads=[B("p0a"), B("p0r")], writes=[B("p0q")])
                    S.op("dve", lambda e, I=I: e.tensor_scalar(
                        p0q.ap()[:, 0:I], p0q.ap()[:, 0:I], RMAGIC, RMAGIC,
                        AL.add, AL.subtract),
                        reads=[B("p0q")], writes=[B("p0q")])
                    sb_ = AP(p0s.ap().tensor, 0, [[96, 128], [1, nob], [0, 32]])
                    S.op("dve", lambda e, sb_=sb_, I=I: e.tensor_tensor(
                        p0b.ap()[:, 0:I].rearrange("p (a b) -> p a b", b=32),
                        p0q.ap()[:, 0:I].rearrange("p (a b) -> p a b", b=32),
                        sb_, AL.mult),
                        reads=[B("p0q"), B("p0s")], writes=[B("p0b")])
                    for it in range(I // 128):
                        S.op("pe", lambda e, it=it: e.transpose(
                            p0ps.ap()[:, 0:128], p0b.ap()[:, 128 * it:128 * (it + 1)],
                            identbf.ap()),
                            reads=[B("p0b"), B("idbf")], writes=[B("p0ps")])
                        S.op("act", lambda e: e.copy(out=p0e.ap(), in_=p0ps.ap()[:, 0:128]),
                             reads=[B("p0ps")], writes=[B("p0e")])
                        d_idx = dst[it, ot] if kt_first else dst[ot, it]
                        S.op("sp", dma(d_idx, p0e.ap()),
                             reads=[B("p0e")], writes=[B("spw")])

            dequant_transpose(w_up, sp_upT, IFF, DM, kt_first=True)
            dequant_transpose(w_gate, sp_gateT, IFF, DM, kt_first=True)
            dequant_transpose(w_down, sp_downT, DM, IFF, kt_first=True)

        ph0_es.close()
        S.barrier()

        pha_es = ctx.enter_context(contextlib.ExitStack())
        ln_x = pha_es.enter_context(nc.sbuf_tensor("ln_x", [128, 4, DM], F32))
        ln_sq = pha_es.enter_context(nc.sbuf_tensor("ln_sq", [128, DM], F32))
        ln_st = pha_es.enter_context(nc.sbuf_tensor("ln_st", [128, 16], F32))
        xn_tok = pha_es.enter_context(nc.sbuf_tensor("xn_tok", [128, 4, DM], BF16))
        xnT_sb = pha_es.enter_context(nc.sbuf_tensor("xnT_sb", [128, NF, TP], BF16))
        wstr = pha_es.enter_context(nc.sbuf_tensor("wstr", [128, 2, NF * 128], BF16))
        xh_t = pha_es.enter_context(nc.sbuf_tensor("xh_t", [128, 2, TP + 3], BF16))
        z_t = pha_es.enter_context(nc.sbuf_tensor("z_t", [128, 2, TP], BF16))
        u_sb = pha_es.enter_context(nc.sbuf_tensor("u_sb", [128, NT, TP], BF16))
        cv_t = pha_es.enter_context(nc.sbuf_tensor("cv_t", [128, 2, TP], BF16))
        sp1 = pha_es.enter_context(nc.sbuf_tensor("sp1", [128, 2, TP], F32))
        dt_t = pha_es.enter_context(nc.sbuf_tensor("dt_t", [128, 2, TP], BF16))
        xdbl = pha_es.enter_context(nc.sbuf_tensor("xdbl", [96, 2, TP], BF16))
        pref = pha_es.enter_context(nc.sbuf_tensor("pref", [128, NT, 3], BF16))
        psA0 = pha_es.enter_context(nc.psum_tensor("psA0", [128, 512], F32))
        psA1 = pha_es.enter_context(nc.psum_tensor("psA1", [128, 512], F32))
        psA2 = pha_es.enter_context(nc.psum_tensor("psA2", [128, 512], F32))
        psA3 = pha_es.enter_context(nc.psum_tensor("psA3", [128, 512], F32))
        psAx = pha_es.enter_context(nc.psum_tensor("psAx", [128, 512], F32))
        psAtr = pha_es.enter_context(nc.psum_tensor("psAtr", [128, 512], BF16))
        wxT_sb = pha_es.enter_context(nc.sbuf_tensor("wxT_sb", [128, 2, NT, 96], BF16))
        wdtT_sb = pha_es.enter_context(nc.sbuf_tensor("wdtT_sb", [DTR, 2, DI], BF16))
        if True:

            psA = [psA0, psA1, psA2, psA3]

            for di, d in enumerate(("f", "r")):
                iv = iv_dir[d]
                for kt in range(NT):
                    S.op("sp", dma(ln_sq.ap()[:, 0:96], iv["w_xT"][128 * kt:128 * (kt + 1), :]),
                         writes=[B("lnsq")])
                    S.op("dve", lambda e, di=di, kt=kt: e.tensor_copy(
                        wxT_sb.ap()[:, di, kt], ln_sq.ap()[:, 0:96]),
                        reads=[B("lnsq")], writes=[B("wxT")])
                stage = AP(ln_x.ap().tensor, 0, [[4 * DM, DTR], [1, DI]])
                S.op("sp", dma(stage, iv["w_dtT"]), writes=[B("lnx0")])
                S.op("dve", lambda e, di=di, stage=stage: e.tensor_copy(
                    wdtT_sb.ap()[:, di], stage),
                    reads=[B("lnx0")], writes=[B("wdtT")])

            def ln_stats_apply(tt):
                xa = ln_x.ap()[:, tt]
                st = ln_st.ap()
                S.op("dve", lambda e, xa=xa: e.reduce_sum(st[:, 0:1], xa, AX.X),
                     reads=[B(f"lnx{tt}")], writes=[B("lnst")])
                S.op("dve", lambda e, xa=xa: e.tensor_tensor(ln_sq.ap(), xa, xa, AL.mult),
                     reads=[B(f"lnx{tt}")], writes=[B("lnsq")])
                S.op("dve", lambda e: e.reduce_sum(st[:, 1:2], ln_sq.ap(), AX.X),
                     reads=[B("lnsq")], writes=[B("lnst")])
                S.op("dve", lambda e: e.tensor_scalar(
                    st[:, 2:3], st[:, 0:1], 1.0 / DM, None, AL.mult),
                    reads=[B("lnst")], writes=[B("lnst")])
                S.op("dve", lambda e: e.tensor_scalar(
                    st[:, 3:4], st[:, 1:2], 1.0 / DM, None, AL.mult),
                    reads=[B("lnst")], writes=[B("lnst")])
                S.op("dve", lambda e: e.tensor_tensor(
                    st[:, 4:5], st[:, 2:3], st[:, 2:3], AL.mult),
                    reads=[B("lnst")], writes=[B("lnst")])
                S.op("dve", lambda e: e.tensor_tensor(
                    st[:, 5:6], st[:, 3:4], st[:, 4:5], AL.subtract),
                    reads=[B("lnst")], writes=[B("lnst")])
                S.op("dve", lambda e: e.tensor_scalar(
                    st[:, 5:6], st[:, 5:6], EPS, None, AL.add),
                    reads=[B("lnst")], writes=[B("lnst")])
                S.op("act", lambda e: e.activation(
                    out=st[:, 6:7], in_=st[:, 5:6], func=AF.Sqrt),
                    reads=[B("lnst")], writes=[B("lnst")])
                S.op("dve", lambda e: e.reciprocal(
                    st[:, 9:10], st[:, 6:7]),
                    reads=[B("lnst")], writes=[B("lnst")])
                S.op("dve", lambda e, xa=xa, tt=tt: e.tensor_scalar(
                    xn_tok.ap()[:, tt], xa, st[:, 2:3], st[:, 9:10],
                    AL.subtract, AL.mult),
                    reads=[B(f"lnx{tt}"), B("lnst")], writes=[B(f"xn{tt}")])
                S.op("dve", lambda e, tt=tt: e.tensor_reduce(
                    st[:, 10:11], xn_tok.ap()[:, tt], AX.X, AL.max,
                    apply_absolute_value=True),
                    reads=[B(f"xn{tt}")], writes=[B("lnst")])
                S.op("dve", lambda e: e.tensor_tensor(
                    xmaxr.ap()[:, 0:1], xmaxr.ap()[:, 0:1], st[:, 10:11], AL.max),
                    reads=[B("lnst")], writes=[B("xmax")])

            def conv_tile(seq, ch, t, par, di):
                sd = 2 * di + seq
                xa = xh_t.ap()[:, par]
                raw_p = [list(x) for x in xa.ap][0]
                if di == 0:
                    shifted = [AP(xa.tensor, xa.offset + j, [raw_p, [1, TP]])
                               for j in range(4)]
                else:
                    shifted = [AP(xa.tensor, xa.offset + (TP + 2) - j,
                                  [raw_p, [-1, TP]]) for j in range(4)]
                S.op("dve", lambda e: e.tensor_scalar(
                    cv_t.ap()[:, par], shifted[0], vconv.ap()[:, di, t, 0:1], None, AL.mult),
                    reads=[B(f"xh{par}"), B("vconv")], writes=[B(f"cv{par}")])
                for j in range(1, 4):
                    S.op("dve", lambda e, j=j: e.scalar_tensor_tensor(
                        cv_t.ap()[:, par], shifted[j], vconv.ap()[:, di, t, j:j + 1],
                        cv_t.ap()[:, par], AL.mult, AL.add),
                        reads=[B(f"xh{par}"), B("vconv"), B(f"cv{par}")],
                        writes=[B(f"cv{par}")])
                S.op("act", lambda e, t=t, par=par, di=di: e.activation(
                    out=u_sb.ap()[:, t], in_=cv_t.ap()[:, par],
                    func=AF.Silu, bias=vconvb.ap()[:, di, t:t + 1]),
                    reads=[B(f"cv{par}"), B("vconvb")], writes=[B(f"u{t}")])
                S.op("sp", dma(sp_u[sd, t, :, ch * TP:(ch + 1) * TP], u_sb.ap()[:, t]),
                     reads=[B(f"u{t}")], writes=[B("sp_u")])

            def ln_and_xz(seq, ch):
                t0 = seq * L + ch * TP
                par = ch % 2
                for tt in range(4):
                    S.op("sp", dma(ln_x.ap()[:, tt],
                                   x_in[t0 + 128 * tt: t0 + 128 * (tt + 1), :]),
                         writes=[B(f"lnx{tt}")])
                for tt in range(4):
                    ln_stats_apply(tt)
                for ft in range(NF):
                    for tt in range(4):
                        S.op("pe", lambda e, ft=ft, tt=tt: e.transpose(
                            psAtr.ap()[:, 128 * tt:128 * (tt + 1)],
                            xn_tok.ap()[:, tt, 128 * ft:128 * (ft + 1)], identbf.ap()),
                            reads=[B(f"xn{tt}"), B("idbf")], writes=[B("psAtr")])
                    S.op("act", lambda e, ft=ft: e.copy(
                        out=xnT_sb.ap()[:, ft], in_=psAtr.ap()),
                        reads=[B("psAtr")], writes=[B(f"xnT{ft}")])
                    S.op("sp", dma(sp_xnT[ft, :, t0:t0 + TP], xnT_sb.ap()[:, ft]),
                         reads=[B(f"xnT{ft}")], writes=[B("sp_xnT")])
                for mt in range(2 * DI // 128):
                    wb = B(f"wstr{mt % 2}")
                    S.op("sp", dma(
                        wstr.ap()[:, mt % 2].rearrange("p (k c) -> p k c", c=128),
                        sp_winT[mt].transpose([1, 0, 2])),
                        reads=[B("spw")], writes=[wb])
                    bank = psA[mt % 4]
                    pb = B(f"psA{mt % 4}")
                    for kt in range(NF):
                        S.op("pe", lambda e, bank=bank, mt=mt, kt=kt: e.matmul(
                            bank.ap(), wstr.ap()[:, mt % 2, 128 * kt:128 * (kt + 1)],
                            xnT_sb.ap()[:, kt], start=(kt == 0), stop=(kt == NF - 1)),
                            reads=[wb, B(f"xnT{kt}")],
                            writes=[pb] if kt in (0, NF - 1) else [])
                    if mt < NT:
                        S.op("act", lambda e, bank=bank, par=par: e.copy(
                            out=xh_t.ap()[:, par, 3:], in_=bank.ap()),
                            reads=[pb], writes=[B(f"xh{par}")])
                        S.op("dve", lambda e, mt=mt, par=par: e.tensor_copy(
                            xh_t.ap()[:, par, 0:3], pref.ap()[:, mt]),
                            reads=[B("pref")], writes=[B(f"xh{par}")])
                        S.op("sp", dma(sp_xh[seq, mt, :, ch * TP:(ch + 1) * TP],
                                       xh_t.ap()[:, par, 3:]),
                             reads=[B(f"xh{par}")], writes=[B("sp_xh")])
                        conv_tile(seq, ch, mt, par, 0)
                        S.op("dve", lambda e, mt=mt, par=par: e.tensor_copy(
                            pref.ap()[:, mt], xh_t.ap()[:, par, TP:TP + 3]),
                            reads=[B(f"xh{par}")], writes=[B("pref")])
                    else:
                        t = mt - NT
                        S.op("act", lambda e, bank=bank, par=par: e.copy(
                            out=z_t.ap()[:, par], in_=bank.ap()),
                            reads=[pb], writes=[B(f"z{par}")])
                        S.op("sp", dma(sp_z[seq, t, :, ch * TP:(ch + 1) * TP],
                                       z_t.ap()[:, par]),
                             reads=[B(f"z{par}")], writes=[B("sp_z")])

            def xdbl_dt(seq, ch, par, di):
                sd = 2 * di + seq
                for kt in range(NT):
                    S.op("pe", lambda e, kt=kt, di=di: e.matmul(
                        psAx.ap()[0:96, :], wxT_sb.ap()[:, di, kt], u_sb.ap()[:, kt],
                        start=(kt == 0), stop=(kt == NT - 1)),
                        reads=[B("wxT"), B(f"u{kt}")],
                        writes=[B("psAx")] if kt in (0, NT - 1) else [])
                S.op("act", lambda e, par=par: e.copy(
                    out=xdbl.ap()[:, par], in_=psAx.ap()[0:96, :]),
                    reads=[B("psAx")], writes=[B(f"xdbl{par}")])
                S.op("sp", dma(sp_B[sd, :, ch * TP:(ch + 1) * TP],
                               xdbl.ap()[64:80, par]),
                     reads=[B(f"xdbl{par}")], writes=[B("sp_B")])
                S.op("sp", dma(sp_C[sd, :, ch * TP:(ch + 1) * TP],
                               xdbl.ap()[80:96, par]),
                     reads=[B(f"xdbl{par}")], writes=[B("sp_C")])
                for mt in range(NT):
                    S.op("pe", lambda e, mt=mt, di=di, par=par: e.matmul(
                        psAx.ap(), wdtT_sb.ap()[:, di, 128 * mt:128 * (mt + 1)],
                        xdbl.ap()[0:DTR, par], start=True, stop=True),
                        reads=[B("wdtT"), B(f"xdbl{par}")], writes=[B("psAx")])
                    S.op("dve", lambda e, mt=mt, di=di, par=par: e.tensor_scalar(
                        sp1.ap()[:, par], psAx.ap(), vbdt.ap()[:, di, mt:mt + 1], 80.0,
                        AL.add, AL.min),
                        reads=[B("psAx"), B("vbdt")], writes=[B(f"sp1{par}")])
                    S.op("act", lambda e, par=par: e.activation(
                        out=sp1.ap()[:, par], in_=sp1.ap()[:, par], func=AF.Exp),
                        reads=[B(f"sp1{par}")], writes=[B(f"sp1{par}")])
                    S.op("act", lambda e, par=par: e.activation(
                        out=dt_t.ap()[:, par], in_=sp1.ap()[:, par], func=AF.Ln, bias=1.0),
                        reads=[B(f"sp1{par}")], writes=[B(f"dtv{par}")])
                    S.op("sp", dma(sp_dt[sd, mt, :, ch * TP:(ch + 1) * TP],
                                   dt_t.ap()[:, par]),
                         reads=[B(f"dtv{par}")], writes=[B("sp_dt")])

            S.op("dve", lambda e: e.memset(xmaxr.ap(), 0.0), writes=[B("xmax")])

            for seq in range(SEQ):
                S.op("dve", lambda e: e.memset(pref.ap(), 0.0), writes=[B("pref")])
                for ch in range(NCHP):
                    ln_and_xz(seq, ch)
                    xdbl_dt(seq, ch, ch % 2, 0)

            def coll(key, srccol):
                cin, cout = ccb[key]
                S.op("pe", lambda e: e.transpose(
                    psAx.ap()[0:1, 0:128], xmaxr.ap()[:, srccol:srccol + 1],
                    ident32.ap()),
                    reads=[B("xmax"), B("id32"), B("psAx")], writes=[B("psAx")])
                S.op("dve", lambda e: e.tensor_reduce(
                    srow.ap()[0:1, 0:1], psAx.ap()[0:1, 0:128],
                    AX.X, AL.max),
                    reads=[B("psAx")], writes=[B("srow")])
                S.op("dve", lambda e: e.tensor_copy(
                    srow.ap()[0:1, 0:128],
                    srow.ap()[0:1, 0:1].broadcast_to([1, 128])),
                    reads=[B("srow")], writes=[B("srow")])
                S.op("sp", dma(cin, srow.ap()[0:1, 0:128]),
                     reads=[B("srow")], writes=[B("cc" + key)])
                S.op("gp", lambda e, cin=cin, cout=cout: e.collective_compute(
                    "AllReduce", AL.max, replica_groups=[list(range(NCORES))],
                    ins=[cin], outs=[cout]),
                    reads=[B("cc" + key)], writes=[B("cco" + key)])

            coll("x", 0)

            for seq in (range(SEQ) if "b" in PHASES else ()):
                for kr in range(NCHP):
                    par = kr % 2
                    o0 = L - (kr + 1) * TP
                    for t in range(NT):
                        ncols = TP + 3 if kr > 0 else TP
                        S.op("sp", dma(xh_t.ap()[:, par, 0:ncols],
                                       sp_xh[seq, t, :, o0:o0 + ncols]),
                             reads=[B("sp_xh")], writes=[B(f"xh{par}")])
                        if kr == 0:
                            S.op("dve", lambda e, par=par: e.memset(
                                xh_t.ap()[:, par, TP:TP + 3], 0.0),
                                writes=[B(f"xh{par}")])
                        conv_tile(seq, kr, t, par, 1)
                    xdbl_dt(seq, kr, par, 1)
            S.barrier()

        pha_es.close()
        phc_es = ctx.enter_context(contextlib.ExitStack())
        brep = phc_es.enter_context(nc.sbuf_tensor("brep", [128, DS, TC], BF16))
        crep = phc_es.enter_context(nc.sbuf_tensor("crep", [128, DS, TC], BF16))
        dtt = phc_es.enter_context(nc.sbuf_tensor("dtt", [128, 2, TC], BF16))
        utt = phc_es.enter_context(nc.sbuf_tensor("utt", [128, 2, TC], BF16))
        wtt = phc_es.enter_context(nc.sbuf_tensor("wtt", [128, 2, TC], BF16))
        ztt = phc_es.enter_context(nc.sbuf_tensor("ztt", [128, TC], BF16))
        ea_in = phc_es.enter_context(nc.sbuf_tensor("ea_in", [128, SW, TC], BF16))
        aa = phc_es.enter_context(nc.sbuf_tensor("aa", [128, SW, TC], F32))
        bb = phc_es.enter_context(nc.sbuf_tensor("bb", [128, SW, TC], BF16))
        hc = phc_es.enter_context(nc.sbuf_tensor("hc", [128, SW, TC], BF16))
        hcc = phc_es.enter_context(nc.sbuf_tensor("hcc", [128, SW, TC], BF16))
        pst = phc_es.enter_context(nc.sbuf_tensor("pst", [128, TC], BF16))
        ytile = phc_es.enter_context(nc.sbuf_tensor("ytile", [128, TC], F32))
        y2t = phc_es.enter_context(nc.sbuf_tensor("y2", [128, TC], F32))
        zst = phc_es.enter_context(nc.sbuf_tensor("zst", [128, TC], BF16))
        yg = phc_es.enter_context(nc.sbuf_tensor("yg", [128, NT, TC], BF16))
        attn_sc = phc_es.enter_context(nc.sbuf_tensor("attn_sc", [128, NF, TC], F32))
        attn_ld = phc_es.enter_context(nc.sbuf_tensor("attn_ld", [128, 512], F32))
        attn_tmp = phc_es.enter_context(nc.sbuf_tensor("attn_tmp", [128, 512], F32))
        wo_sb = phc_es.enter_context(nc.sbuf_tensor("wo_sb", [128, 2, NT * 128], BF16))
        carry = phc_es.enter_context(nc.sbuf_tensor("carry", [128, NT, DS], F32))
        psC0 = phc_es.enter_context(nc.psum_tensor("psC0", [128, 512], F32))
        psC1 = phc_es.enter_context(nc.psum_tensor("psC1", [128, 512], F32))
        psC2 = phc_es.enter_context(nc.psum_tensor("psC2", [128, 512], F32))
        psC3 = phc_es.enter_context(nc.psum_tensor("psC3", [128, 512], F32))
        if True:

            psC = [psC0, psC1, psC2, psC3]

            sd_sel = os.environ.get("SD_ONLY")
            sd_list = [int(x) for x in sd_sel.split(",")] if sd_sel else (list(range(4)) if "c" in PHASES else [])
            for sd in sd_list:
                di, seq = sd // 2, sd % 2
                S.op("dve", lambda e: e.memset(carry.ap(), 0.0), writes=[B("carry")])
                for sc in range(NSC):
                    if os.environ.get("PHC_BARRIERS"):
                        S.barrier()
                    c0 = sc * TC
                    for rep, spsrc, dstt, bn in ((0, sp_B, brep, "brep"), (1, sp_C, crep, "crep")):
                        srcsl = spsrc[sd, :, c0:c0 + TC]
                        raw = [list(x) for x in srcsl.ap]
                        bc = AP(srcsl.tensor, srcsl.offset,
                                [[0, 128]] + raw)
                        S.op("sp", dma(dstt.ap(), bc),
                             reads=[B("sp_B" if rep == 0 else "sp_C")],
                             writes=[B(bn)])

                    for t in range(NT):
                        tp = t % 2
                        S.op("sp", dma(dtt.ap()[:, tp], sp_dt[sd, t, :, c0:c0 + TC]),
                             reads=[B("sp_dt")], writes=[B(f"dtt{tp}")])
                        S.op("sp", dma(utt.ap()[:, tp], sp_u[sd, t, :, c0:c0 + TC]),
                             reads=[B("sp_u")], writes=[B(f"utt{tp}")])
                        S.op("dve", lambda e, tp=tp: e.tensor_tensor(
                            wtt.ap()[:, tp], dtt.ap()[:, tp], utt.ap()[:, tp], AL.mult),
                            reads=[B(f"dtt{tp}"), B(f"utt{tp}")], writes=[B(f"wtt{tp}")])
                        for sg in range(SG):
                            s0 = SW * sg
                            dbc = dtt.ap()[:, tp].unsqueeze(1).broadcast_to([128, SW, TC])
                            eab = veA.ap()[:, di, t, s0:s0 + SW].unsqueeze(2).broadcast_to([128, SW, TC])
                            S.op("dve", lambda e, dbc=dbc, eab=eab: e.tensor_tensor(
                                ea_in.ap(), dbc, eab, AL.mult),
                                reads=[B(f"dtt{tp}"), B("veA")], writes=[B("eain")])
                            S.op("act", lambda e: e.activation(
                                out=aa.ap(), in_=ea_in.ap(), func=AF.Exp, scale=-1.0),
                                reads=[B("eain")], writes=[B("aa")])
                            wbc = wtt.ap()[:, tp].unsqueeze(1).broadcast_to([128, SW, TC])
                            S.op("dve", lambda e, wbc=wbc, s0=s0: e.tensor_tensor(
                                bb.ap(), wbc, brep.ap()[:, s0:s0 + SW], AL.mult),
                                reads=[B(f"wtt{tp}"), B("brep")], writes=[B("bbb")])
                            for j in range(SW):
                                init = 0.0 if sc == 0 else carry.ap()[:, t, s0 + j:s0 + j + 1]
                                S.op("dve", lambda e, j=j, init=init: e.tensor_tensor_scan(
                                    out=hc.ap()[:, j], data0=aa.ap()[:, j],
                                    data1=bb.ap()[:, j], initial=init,
                                    op0=AL.mult, op1=AL.add),
                                    reads=[B("aa"), B("bbb")] + ([B("carry")] if sc else []),
                                    writes=[B("hc")])
                            S.op("dve", lambda e, s0=s0: e.tensor_tensor(
                                hcc.ap(), hc.ap(), crep.ap()[:, s0:s0 + SW], AL.mult),
                                reads=[B("hc"), B("crep")], writes=[B("hcc")])
                            S.op("dve", lambda e: e.tensor_tensor(
                                pst.ap(), hcc.ap()[:, 0], hcc.ap()[:, 1], AL.add),
                                reads=[B("hcc")], writes=[B("pst")])
                            if sg == 0:
                                S.op("dve", lambda e: e.tensor_copy(ytile.ap(), pst.ap()),
                                     reads=[B("pst")], writes=[B("ytile")])
                            else:
                                S.op("dve", lambda e: e.tensor_tensor(
                                    ytile.ap(), ytile.ap(), pst.ap(), AL.add),
                                    reads=[B("pst"), B("ytile")], writes=[B("ytile")])
                            S.op("dve", lambda e, t=t, s0=s0: e.tensor_copy(
                                carry.ap()[:, t, s0:s0 + SW],
                                AP(hc.ap().tensor, TC - 1, [[SW * TC, 128], [TC, SW], [1, 1]])),
                                reads=[B("hc")], writes=[B("carry")])
                        if di == 0:
                            S.op("sp", dma(ztt.ap(), sp_z[seq, t, :, c0:c0 + TC]),
                                 reads=[B("sp_z")], writes=[B("ztt")])
                            zin = ztt.ap()
                        else:
                            o0 = L - (sc + 1) * TC
                            S.op("sp", dma(ztt.ap(), sp_z[seq, t, :, o0:o0 + TC]),
                                 reads=[B("sp_z")], writes=[B("ztt")])
                            zin = rev_free(ztt.ap())
                        S.op("dve", lambda e, t=t, tp=tp, di=di: e.scalar_tensor_tensor(
                            y2t.ap(), utt.ap()[:, tp], vD.ap()[:, di, t:t + 1],
                            ytile.ap(), AL.mult, AL.add),
                            reads=[B(f"utt{tp}"), B("vD"), B("ytile")], writes=[B("y2")])
                        S.op("act", lambda e, zin=zin: e.activation(
                            out=zst.ap(), in_=zin, func=AF.Silu),
                            reads=[B("ztt")], writes=[B("zst")])
                        S.op("dve", lambda e, t=t: e.tensor_tensor(
                            yg.ap()[:, t], y2t.ap(), zst.ap(), AL.mult),
                            reads=[B("y2"), B("zst")], writes=[B(f"yg{t}")])
                        if os.environ.get("TAP_YG"):
                            S.op("sp", dma(sp_yg[sd, t, :, c0:c0 + TC], yg.ap()[:, t]),
                                 reads=[B(f"yg{t}")], writes=[B("sp_yg")])

                    for mt in range(NF):
                        wb = B(f"wo{mt % 2}")
                        S.op("sp", dma(
                            wo_sb.ap()[:, mt % 2].rearrange("p (k c) -> p k c", c=128),
                            sp_woutT[mt].transpose([1, 0, 2])),
                            reads=[B("spw")], writes=[wb])
                        for half in range(TC // 512):
                            bank = psC[(2 * mt + half) % 4]
                            pb = B(f"psC{(2 * mt + half) % 4}")
                            for kt in range(NT):
                                S.op("pe", lambda e, bank=bank, mt=mt, kt=kt, half=half: e.matmul(
                                    bank.ap(),
                                    wo_sb.ap()[:, mt % 2, 128 * kt:128 * (kt + 1)],
                                    yg.ap()[:, kt, 512 * half:512 * (half + 1)],
                                    start=(kt == 0), stop=(kt == NT - 1)),
                                    reads=[wb, B(f"yg{kt}")],
                                    writes=[pb] if kt in (0, NT - 1) else [])
                            if di == 0:
                                S.op("act", lambda e, bank=bank, mt=mt, half=half: e.copy(
                                    out=attn_sc.ap()[:, mt, 512 * half:512 * (half + 1)],
                                    in_=bank.ap()),
                                    reads=[pb], writes=[B(f"attn{mt}")])
                            else:
                                o0 = L - sc * TC - (half + 1) * 512
                                S.op("sp", dma(attn_ld.ap(), sp_attn[seq, mt, :, o0:o0 + 512]),
                                     reads=[B("sp_attn")], writes=[B("attnld")])
                                S.op("act", lambda e, bank=bank: e.copy(
                                    out=attn_tmp.ap(), in_=bank.ap()),
                                    reads=[pb], writes=[B("attntmp")])
                                S.op("dve", lambda e, mt=mt, half=half: e.tensor_tensor(
                                    attn_sc.ap()[:, mt, 512 * half:512 * (half + 1)],
                                    rev_free(attn_tmp.ap()), attn_ld.ap(), AL.add),
                                    reads=[B("attntmp"), B("attnld")], writes=[B(f"attn{mt}")])
                    for mt in range(NF):
                        if di == 0:
                            dst = sp_attn[seq, mt, :, c0:c0 + TC]
                            S.op("sp", dma(dst, attn_sc.ap()[:, mt]),
                                 reads=[B(f"attn{mt}")], writes=[B("sp_attn")])
                        else:
                            for half in range(TC // 512):
                                o0 = L - sc * TC - (half + 1) * 512
                                S.op("sp", dma(sp_attn[seq, mt, :, o0:o0 + 512],
                                               attn_sc.ap()[:, mt, 512 * half:512 * (half + 1)]),
                                     reads=[B(f"attn{mt}")], writes=[B("sp_attn")])
            S.barrier()

        phc_es.close()
        ffn_es = ctx.enter_context(contextlib.ExitStack())
        fx = ffn_es.enter_context(nc.sbuf_tensor("fx", [128, NF, TP], BF16))
        fxq = ffn_es.enter_context(nc.sbuf_tensor("fxq", [128, NF, TP], BF16))
        ftmp = ffn_es.enter_context(nc.sbuf_tensor("ftmp", [128, TP], F32))
        ftmp2 = ffn_es.enter_context(nc.sbuf_tensor("ftmp2", [128, TP], F32))
        ftmp3 = ffn_es.enter_context(nc.sbuf_tensor("ftmp3", [128, TP], F32))
        fbf = ffn_es.enter_context(nc.sbuf_tensor("fbf", [128, 2, TP], BF16))
        fw = ffn_es.enter_context(nc.sbuf_tensor("fw", [128, 2, NTI * 128], BF16))
        foq = ffn_es.enter_context(nc.sbuf_tensor("foq", [128, NTI, TP], BF16))
        fmax = ffn_es.enter_context(nc.sbuf_tensor("fmax", [128, 4], F32))
        attn_ld2 = ffn_es.enter_context(nc.sbuf_tensor("attn_ld2", [128, TP], F32))
        sumFM = ffn_es.enter_context(nc.sbuf_tensor("sumFM", [128, NF, TP], F32))
        xres = ffn_es.enter_context(nc.sbuf_tensor("xres", [128, DM], F32))
        outsb = ffn_es.enter_context(nc.sbuf_tensor("outsb", [128, DM], F32))
        psF0 = ffn_es.enter_context(nc.psum_tensor("psF0", [128, 512], F32))
        psF1 = ffn_es.enter_context(nc.psum_tensor("psF1", [128, 512], F32))
        psF2 = ffn_es.enter_context(nc.psum_tensor("psF2", [128, 512], F32))
        psF3 = ffn_es.enter_context(nc.psum_tensor("psF3", [128, 512], F32))
        psFt = ffn_es.enter_context(nc.psum_tensor("psFt", [128, 512], F32))
        if True:

            psF = [psF0, psF1, psF2, psF3]
            NCH_F = TOK // TP

            def bcast_scalar_from(key, col):
                _, cout = ccb[key]
                S.op("sp", dma(scal.ap()[:, col:col + 1],
                               cout.rearrange("o p -> (o p)").unsqueeze(1)),
                     reads=[B("cco" + key)], writes=[B("scal")])

            def qdq_scales(gcol, scol, icol):
                S.op("dve", lambda e: e.tensor_scalar(
                    scal.ap()[:, scol:scol + 1], scal.ap()[:, gcol:gcol + 1],
                    1.0 / QMAX, 1e-12, AL.mult, AL.max),
                    reads=[B("scal")], writes=[B("scal")])
                S.op("dve", lambda e: e.reciprocal(
                    scal.ap()[:, icol:icol + 1], scal.ap()[:, scol:scol + 1]),
                    reads=[B("scal")], writes=[B("scal")])

            FF = "f" in PHASES
            if FF:
                bcast_scalar_from("x", 0)
            if FF:
                qdq_scales(0, 1, 2)
                S.op("dve", lambda e: e.memset(fmax.ap(), 0.0), writes=[B("fmax")])

            for chf in (range(NCH_F) if FF and FFSUB >= 1 else ()):
                t0 = chf * TP
                for ft in range(NF):
                    S.op("sp", dma(fx.ap()[:, ft], sp_xnT[ft, :, t0:t0 + TP]),
                         reads=[B("sp_xnT")], writes=[B(f"fx{ft}")])
                    S.op("dve", lambda e, ft=ft: e.tensor_scalar(
                        ftmp.ap(), fx.ap()[:, ft], scal.ap()[:, 2:3], RMAGIC,
                        AL.mult, AL.add),
                        reads=[B(f"fx{ft}"), B("scal")], writes=[B("ftmp")])
                    S.op("dve", lambda e, ft=ft: e.tensor_scalar(
                        fxq.ap()[:, ft], ftmp.ap(), RMAGIC, scal.ap()[:, 1:2],
                        AL.subtract, AL.mult),
                        reads=[B("ftmp"), B("scal")], writes=[B(f"fxq{ft}")])
                for mat, spT, spraw, maxcol in ((0, sp_upT, sp_up, 0), (1, sp_gateT, sp_gate, 1)):
                    if P1MODE < 2:
                        continue
                    for mt in range(NTI):
                        wb = B(f"fw{mt % 2}")
                        S.op("sp", dma(
                            fw.ap()[:, mt % 2, 0:NF * 128].rearrange("p (k c) -> p k c", c=128),
                            spT[:, mt].transpose([1, 0, 2])),
                            reads=[B("spw")], writes=[wb])
                        bank = psF[mt % 4]
                        pb = B(f"psF{mt % 4}")
                        for kt in range(NF):
                            S.op("pe", lambda e, bank=bank, mt=mt, kt=kt: e.matmul(
                                bank.ap(), fw.ap()[:, mt % 2, 128 * kt:128 * (kt + 1)],
                                fxq.ap()[:, kt], start=(kt == 0), stop=(kt == NF - 1)),
                                reads=[wb, B(f"fxq{kt}")],
                                writes=[pb] if kt in (0, NF - 1) else [])
                        if P1MODE < 3:
                            continue
                        S.op("act", lambda e, bank=bank: e.copy(
                            out=fbf.ap()[:, 0], in_=bank.ap()),
                            reads=[pb], writes=[B("fbf0")])
                        if P1MODE >= 4:
                            S.op("dve", lambda e: e.tensor_reduce(
                                fmax.ap()[:, 2:3], fbf.ap()[:, 0], AX.X, AL.max,
                                apply_absolute_value=True),
                                reads=[B("fbf0")], writes=[B("fmaxt")])
                            S.op("dve", lambda e, maxcol=maxcol: e.tensor_tensor(
                                fmax.ap()[:, maxcol:maxcol + 1], fmax.ap()[:, maxcol:maxcol + 1],
                                fmax.ap()[:, 2:3], AL.max),
                                reads=[B("fmaxt")], writes=[B("fmax")])
                        S.op("sp", dma(spraw[mt, :, t0:t0 + TP], fbf.ap()[:, 0]),
                             reads=[B("fbf0")],
                             writes=[B("sp_up" if mat == 0 else "sp_gate")])

            def coll_from_fmax(key, col):
                cin, cout = ccb[key]
                S.op("pe", lambda e, col=col: e.transpose(
                    psFt.ap()[0:1, 0:128], fmax.ap()[:, col:col + 1], ident32.ap()),
                    reads=[B("fmax"), B("id32"), B("psFt")], writes=[B("psFt")])
                S.op("dve", lambda e: e.tensor_reduce(
                    srow.ap()[0:1, 0:1], psFt.ap()[0:1, 0:128], AX.X, AL.max),
                    reads=[B("psFt")], writes=[B("srow")])
                S.op("dve", lambda e: e.tensor_copy(
                    srow.ap()[0:1, 0:128],
                    srow.ap()[0:1, 0:1].broadcast_to([1, 128])),
                    reads=[B("srow")], writes=[B("srow")])
                S.op("sp", dma(cin, srow.ap()[0:1, 0:128]),
                     reads=[B("srow")], writes=[B("cc" + key)])
                S.op("gp", lambda e, cin=cin, cout=cout: e.collective_compute(
                    "AllReduce", AL.max, replica_groups=[list(range(NCORES))],
                    ins=[cin], outs=[cout]),
                    reads=[B("cc" + key)], writes=[B("cco" + key)])

            if FF and FFSUB >= 2:
                coll_from_fmax("ug", 0)
                coll_from_fmax("gg", 1)
                bcast_scalar_from("ug", 3)
                qdq_scales(3, 4, 5)
                bcast_scalar_from("gg", 6)
                qdq_scales(6, 7, 8)

            if FF:
                S.op("dve", lambda e: e.tensor_scalar(
                    fmax.ap()[:, 3:4], fmax.ap()[:, 3:4], 0.0, None, AL.mult),
                    reads=[B("fmax")], writes=[B("fmax")])
            for chf in (range(NCH_F) if FF and FFSUB >= 3 else ()):
                t0 = chf * TP
                for mt in range(NTI):
                    S.op("sp", dma(fbf.ap()[:, 0], sp_gate[mt, :, t0:t0 + TP]),
                         reads=[B("sp_gate")], writes=[B("fbf0")])
                    S.op("dve", lambda e: e.tensor_scalar(
                        ftmp.ap(), fbf.ap()[:, 0], scal.ap()[:, 8:9], RMAGIC,
                        AL.mult, AL.add),
                        reads=[B("fbf0"), B("scal")], writes=[B("ftmp")])
                    S.op("dve", lambda e: e.tensor_scalar(
                        ftmp.ap(), ftmp.ap(), RMAGIC, None, AL.subtract),
                        reads=[B("ftmp")], writes=[B("ftmp")])
                    S.op("act", lambda e: e.activation(
                        out=ftmp2.ap(), in_=ftmp.ap(), func=AF.Sigmoid,
                        scale=scal.ap()[:, 7:8]),
                        reads=[B("ftmp"), B("scal")], writes=[B("ftmp2")])
                    S.op("dve", lambda e: e.tensor_scalar(
                        ftmp2.ap(), ftmp2.ap(), 65536.0, RMAGIC, AL.mult, AL.add),
                        reads=[B("ftmp2")], writes=[B("ftmp2")])
                    S.op("dve", lambda e: e.tensor_scalar(
                        ftmp2.ap(), ftmp2.ap(), RMAGIC, 65535.0, AL.subtract, AL.min),
                        reads=[B("ftmp2")], writes=[B("ftmp2")])
                    S.op("dve", lambda e: e.tensor_tensor(
                        fbf.ap()[:, 1], ftmp.ap(), ftmp2.ap(), AL.mult),
                        reads=[B("ftmp"), B("ftmp2")], writes=[B("fbf1")])
                    S.op("dve", lambda e: e.tensor_reduce(
                        fmax.ap()[:, 2:3], fbf.ap()[:, 1], AX.X, AL.max,
                        apply_absolute_value=True),
                        reads=[B("fbf1")], writes=[B("fmaxt")])
                    S.op("dve", lambda e: e.tensor_tensor(
                        fmax.ap()[:, 3:4], fmax.ap()[:, 3:4], fmax.ap()[:, 2:3], AL.max),
                        reads=[B("fmaxt")], writes=[B("fmax")])
                    S.op("sp", dma(sp_m[mt, :, t0:t0 + TP], fbf.ap()[:, 1]),
                         reads=[B("fbf1")], writes=[B("sp_m")])
            if FF and FFSUB >= 4:
                S.op("dve", lambda e: e.tensor_tensor(
                    fmax.ap()[:, 3:4], fmax.ap()[:, 3:4], scal.ap()[:, 7:8], AL.mult),
                    reads=[B("fmax"), B("scal")], writes=[B("fmax")])
                S.op("dve", lambda e: e.tensor_scalar(
                    fmax.ap()[:, 3:4], fmax.ap()[:, 3:4], 1.0 / 65536.0, None, AL.mult),
                    reads=[B("fmax")], writes=[B("fmax")])
                coll_from_fmax("m", 3)
                bcast_scalar_from("m", 9)
                qdq_scales(9, 10, 11)
                S.op("dve", lambda e: e.tensor_tensor(
                    scal.ap()[:, 12:13], scal.ap()[:, 7:8], scal.ap()[:, 11:12], AL.mult),
                    reads=[B("scal")], writes=[B("scal")])
                S.op("dve", lambda e: e.tensor_scalar(
                    scal.ap()[:, 12:13], scal.ap()[:, 12:13], 1.0 / 65536.0, None, AL.mult),
                    reads=[B("scal")], writes=[B("scal")])

                S.op("dve", lambda e: e.tensor_scalar(
                    fmax.ap()[:, 3:4], fmax.ap()[:, 3:4], 0.0, None, AL.mult),
                    reads=[B("fmax")], writes=[B("fmax")])
            for chf in (range(NCH_F) if FF and FFSUB >= 4 else ()):
                t0 = chf * TP
                for mt in range(NTI):
                    S.op("sp", dma(fbf.ap()[:, 0], sp_m[mt, :, t0:t0 + TP]),
                         reads=[B("sp_m")], writes=[B("fbf0")])
                    S.op("sp", dma(fbf.ap()[:, 1], sp_up[mt, :, t0:t0 + TP]),
                         reads=[B("sp_up")], writes=[B("fbf1")])
                    S.op("dve", lambda e: e.tensor_scalar(
                        ftmp.ap(), fbf.ap()[:, 0], scal.ap()[:, 12:13], RMAGIC,
                        AL.mult, AL.add),
                        reads=[B("fbf0"), B("scal")], writes=[B("ftmp")])
                    S.op("dve", lambda e: e.tensor_scalar(
                        ftmp.ap(), ftmp.ap(), RMAGIC, None, AL.subtract),
                        reads=[B("ftmp")], writes=[B("ftmp")])
                    S.op("dve", lambda e: e.tensor_scalar(
                        ftmp2.ap(), fbf.ap()[:, 1], scal.ap()[:, 5:6], RMAGIC,
                        AL.mult, AL.add),
                        reads=[B("fbf1"), B("scal")], writes=[B("ftmp2")])
                    S.op("dve", lambda e: e.tensor_scalar(
                        ftmp2.ap(), ftmp2.ap(), RMAGIC, None, AL.subtract),
                        reads=[B("ftmp2")], writes=[B("ftmp2")])
                    S.op("dve", lambda e: e.tensor_tensor(
                        fbf.ap()[:, 0], ftmp.ap(), ftmp2.ap(), AL.mult),
                        reads=[B("ftmp"), B("ftmp2")], writes=[B("fbf0")])
                    S.op("dve", lambda e: e.tensor_reduce(
                        fmax.ap()[:, 2:3], fbf.ap()[:, 0], AX.X, AL.max,
                        apply_absolute_value=True),
                        reads=[B("fbf0")], writes=[B("fmaxt")])
                    S.op("dve", lambda e: e.tensor_tensor(
                        fmax.ap()[:, 3:4], fmax.ap()[:, 3:4], fmax.ap()[:, 2:3], AL.max),
                        reads=[B("fmaxt")], writes=[B("fmax")])
                    S.op("sp", dma(sp_t[mt, :, t0:t0 + TP], fbf.ap()[:, 0]),
                         reads=[B("fbf0")], writes=[B("sp_t")])
            if FF and FFSUB >= 5:
                S.op("dve", lambda e: e.tensor_tensor(
                    fmax.ap()[:, 3:4], fmax.ap()[:, 3:4], scal.ap()[:, 10:11], AL.mult),
                    reads=[B("fmax"), B("scal")], writes=[B("fmax")])
                S.op("dve", lambda e: e.tensor_tensor(
                    fmax.ap()[:, 3:4], fmax.ap()[:, 3:4], scal.ap()[:, 4:5], AL.mult),
                    reads=[B("fmax"), B("scal")], writes=[B("fmax")])
                coll_from_fmax("t", 3)
                bcast_scalar_from("t", 13)
                qdq_scales(13, 14, 15)
                S.op("dve", lambda e: e.tensor_tensor(
                    scal.ap()[:, 16:17], scal.ap()[:, 10:11], scal.ap()[:, 4:5], AL.mult),
                    reads=[B("scal")], writes=[B("scal")])
                S.op("dve", lambda e: e.tensor_tensor(
                    scal.ap()[:, 16:17], scal.ap()[:, 16:17], scal.ap()[:, 15:16], AL.mult),
                    reads=[B("scal")], writes=[B("scal")])

            for chf in (range(NCH_F) if FF and FFSUB >= 5 else ()):
                t0 = chf * TP
                seq = chf // NCHP
                for mt in range(NTI):
                    S.op("sp", dma(fbf.ap()[:, 0], sp_t[mt, :, t0:t0 + TP]),
                         reads=[B("sp_t")], writes=[B("fbf0")])
                    S.op("dve", lambda e: e.tensor_scalar(
                        ftmp.ap(), fbf.ap()[:, 0], scal.ap()[:, 16:17], RMAGIC,
                        AL.mult, AL.add),
                        reads=[B("fbf0"), B("scal")], writes=[B("ftmp")])
                    S.op("dve", lambda e, mt=mt: e.tensor_scalar(
                        foq.ap()[:, mt], ftmp.ap(), RMAGIC, None, AL.subtract),
                        reads=[B("ftmp")], writes=[B(f"foq{mt}")])
                for mt in range(NF):
                    wb = B(f"fw{mt % 2}")
                    S.op("sp", dma(
                        fw.ap()[:, mt % 2, 0:NTI * 128].rearrange("p (k c) -> p k c", c=128),
                        sp_downT[:, mt].transpose([1, 0, 2])),
                        reads=[B("spw")], writes=[wb])
                    bank = psF[mt % 4]
                    pb = B(f"psF{mt % 4}")
                    for kt in range(NTI):
                        S.op("pe", lambda e, bank=bank, mt=mt, kt=kt: e.matmul(
                            bank.ap(), fw.ap()[:, mt % 2, 128 * kt:128 * (kt + 1)],
                            foq.ap()[:, kt], start=(kt == 0), stop=(kt == NTI - 1)),
                            reads=[wb, B(f"foq{kt}")],
                            writes=[pb] if kt in (0, NTI - 1) else [])
                    ch_in_seq = chf % NCHP
                    a0 = ch_in_seq * TP
                    S.op("sp", dma(attn_ld2.ap(), sp_attn[seq, mt, :, a0:a0 + TP]),
                         reads=[B("sp_attn")], writes=[B("attnld2")])
                    S.op("dve", lambda e, bank=bank, mt=mt: e.tensor_scalar(
                        sumFM.ap()[:, mt], bank.ap(), scal.ap()[:, 14:15], None, AL.mult),
                        reads=[pb, B("scal")], writes=[B(f"sum{mt}")])
                    S.op("dve", lambda e, mt=mt: e.tensor_tensor(
                        sumFM.ap()[:, mt], sumFM.ap()[:, mt], attn_ld2.ap(), AL.add),
                        reads=[B(f"sum{mt}"), B("attnld2")], writes=[B(f"sum{mt}")])
                for tt in range(4):
                    for fhalf in range(2):
                        bank = psF[fhalf]
                        pb = B(f"psF{fhalf}")
                        for ft4 in range(4):
                            ft = 4 * fhalf + ft4
                            S.op("pe", lambda e, bank=bank, ft=ft, ft4=ft4, tt=tt: e.transpose(
                                bank.ap()[:, 128 * ft4:128 * (ft4 + 1)],
                                sumFM.ap()[:, ft, 128 * tt:128 * (tt + 1)], ident32.ap()),
                                reads=[B(f"sum{ft}"), B("id32")],
                                writes=[pb] if ft4 in (0, 3) else [])
                    S.op("sp", dma(xres.ap(), x_in[t0 + 128 * tt:t0 + 128 * (tt + 1), :]),
                         writes=[B("xres")])
                    for fhalf in range(2):
                        bank = psF[fhalf]
                        pb = B(f"psF{fhalf}")
                        S.op("dve", lambda e, bank=bank, fhalf=fhalf: e.tensor_tensor(
                            outsb.ap()[:, 512 * fhalf:512 * (fhalf + 1)], bank.ap(),
                            xres.ap()[:, 512 * fhalf:512 * (fhalf + 1)], AL.add),
                            reads=[pb, B("xres")], writes=[B("outsb")])
                    S.op("sp", dma(out[t0 + 128 * tt:t0 + 128 * (tt + 1), :], outsb.ap()),
                         reads=[B("outsb")], writes=[B("outd")])

        for name in DBG:
            srct = spmap[name]
            S.op("sp", dma(dbg_out[name], srct), reads=[B("sp_" + name)])
        if "attn" in DBG and os.environ.get("DUMP2"):
            dbg_out["attn2"] = nc.dram_tensor("dbg_attn2", list(sp_attn.shape),
                                              sp_attn.dtype, kind="ExternalOutput").ap()
            S.op("sp", dma(dbg_out["attn2"], sp_attn), reads=[B("sp_attn")])

        S.final_wait("sp", "sp")
        with nc.Block() as block:
            S.emit(block, sems)
    return nc


def _shard_inputs(inputs):
    x = np.ascontiguousarray(inputs["L0_attn_hidden_states"], dtype=np.float32)
    base = {
        "w_inT": np.ascontiguousarray(np.asarray(inputs["L0_attn_W_in"]).T, np.float32),
        "w_outT": np.ascontiguousarray(np.asarray(inputs["L0_attn_W_out"]).T, np.float32),
        "w_gate": np.ascontiguousarray(np.asarray(inputs["L0_ffn_w_gate"]), np.float32),
        "w_up": np.ascontiguousarray(np.asarray(inputs["L0_ffn_w_up"]), np.float32),
        "w_down": np.ascontiguousarray(np.asarray(inputs["L0_ffn_w_down"]), np.float32),
    }
    for d in ("f", "r"):
        base[f"w_xT_{d}"] = np.ascontiguousarray(np.asarray(inputs[f"L0_attn_W_x_{d}"]).T, np.float32)
        base[f"w_dtT_{d}"] = np.ascontiguousarray(np.asarray(inputs[f"L0_attn_W_dt_{d}"]).T, np.float32)
        base[f"conv_w_{d}"] = np.ascontiguousarray(np.asarray(inputs[f"L0_attn_conv_w_{d}"]), np.float32)
        base[f"conv_b_{d}"] = np.asarray(inputs[f"L0_attn_conv_b_{d}"], np.float32).reshape(DI, 1)
        base[f"b_dt_{d}"] = np.asarray(inputs[f"L0_attn_b_dt_{d}"], np.float32).reshape(DI, 1)
        base[f"A_log_{d}"] = np.ascontiguousarray(np.asarray(inputs[f"L0_attn_A_log_{d}"]), np.float32)
        base[f"D_{d}"] = np.asarray(inputs[f"L0_attn_D_{d}"], np.float32).reshape(DI, 1)
    per = []
    for c in range(NCORES):
        m = dict(base)
        m["x_in"] = np.ascontiguousarray(
            x[SEQ * c:SEQ * (c + 1)].reshape(TOK, DM), np.float32)
        per.append(m)
    return per


def kernel(**inputs):
    per = _shard_inputs(inputs)
    nc = build_program()
    res = run_bass_kernel_spmd(nc, per, core_ids=list(range(NCORES)))
    outs = [res.results[c]["out"].reshape(SEQ, L, DM) for c in range(NCORES)]
    return np.concatenate(outs, axis=0)
